# version 02 (final)
# baseline (speedup 1.0000x reference)
import numpy as np
import ml_dtypes
from contextlib import ExitStack
import concourse.bass as bass
import concourse.mybir as mybir
from concourse.bass_utils import run_bass_kernel_spmd

F32 = mybir.dt.float32
BF16 = mybir.dt.bfloat16
AF = mybir.ActivationFunctionType
OP = mybir.AluOpType
AX = mybir.AxisListType

NCORE = 8
DM = 1024
TT = 16384
TC = TT // NCORE
HU = 4096
LNEPS = 1e-5
BF = ml_dtypes.bfloat16


class P:
    def __init__(self, nc):
        self.nc = nc
        self.ops = []

    def d(self, fn):
        self.ops.append(('d', 16, fn))

    def v(self, fn):
        self.ops.append(('v', 1, fn))

    def s(self, fn):
        self.ops.append(('s', 1, fn))

    def t(self, fn):
        self.ops.append(('t', 1, fn))


def finish(nc, prog):
    pre = []
    tot = 0
    for e, dl, f in prog.ops:
        pre.append(tot)
        tot += dl
    grand = tot
    with nc.semaphore() as sem, nc.Block() as blk:
        def mk(name):
            idxs = [i for i, o in enumerate(prog.ops) if o[0] == name]

            def body(engine):
                for i in idxs:
                    e, dl, fn = prog.ops[i]
                    skip = i > 0 and prog.ops[i - 1][0] == name and name in ('t', 'd')
                    if (not skip) and pre[i] > 0:
                        engine.wait_ge(sem, pre[i])
                    fn().then_inc(sem, dl)
                engine.wait_ge(sem, grand)
            return body
        blk.sync(mk('d'))
        blk.vector(mk('v'))
        blk.scalar(mk('s'))
        blk.tensor(mk('t'))
    return nc


def emit_mm(pr, nc, env, wT, K, M, CH, rhs_fn, out_fn, mlist=None, bank0=0, nbank=8):
    wst, wbf, PPs = env['wst'], env['wbf'], env['PPs']
    nkb = K // 128
    mts = (M + 127) // 128
    subs = CH // 512
    mgs = max(1, nbank // subs)
    if mlist is None:
        mlist = list(range(mts))
    W = mgs * 128
    for g0 in range(0, len(mlist), mgs):
        mg = mlist[g0:g0 + mgs]
        w0 = mg[0] * 128
        wwid = min(M, (mg[-1] + 1) * 128) - w0
        for kb0 in range(0, nkb, 8):
            kbs = list(range(kb0, min(kb0 + 8, nkb)))
            for j, kb in enumerate(kbs):
                pr.d(lambda kb=kb, j=j, w0=w0, wwid=wwid: nc.sync.dma_start(
                    out=wst[:, j * W: j * W + wwid],
                    in_=wT[kb * 128:(kb + 1) * 128, w0:w0 + wwid]))
            for j in range(len(kbs)):
                pr.v(lambda j=j, wwid=wwid: nc.vector.tensor_copy(
                    out=wbf[:, j * W: j * W + wwid],
                    in_=wst[:, j * W: j * W + wwid]))
            for mi, mt in enumerate(mg):
                mwid = min(128, M - mt * 128)
                for sub in range(subs):
                    bank = bank0 + mi * subs + sub
                    for j, kb in enumerate(kbs):
                        pr.t(lambda mi=mi, sub=sub, j=j, kb=kb, mwid=mwid, bank=bank:
                             nc.tensor.matmul(
                                 out=PPs[bank][0:mwid, 0:512],
                                 lhsT=wbf[:, j * W + mi * 128: j * W + mi * 128 + mwid],
                                 rhs=rhs_fn(kb, sub),
                                 start=(kb == 0), stop=(kb == nkb - 1)))
        for mi, mt in enumerate(mg):
            mwid = min(128, M - mt * 128)
            for sub in range(subs):
                bank = bank0 + mi * subs + sub
                out_fn(mt, sub, PPs[bank][0:mwid, 0:512])


def emit_ln(pr, nc, env, x_sb, CH, xo_f32=None, xo_bf=None, gcol=None, bcol=None):
    PPs = env['PPs']
    ones128 = env['ones128']
    ones1 = env['ones1']
    mrow, vrow, rrow = env['mrow'], env['vrow'], env['rrow']
    scr = env['scr']
    subs = (CH + 511) // 512
    for sub in range(subs):
        w = min(512, CH - sub * 512)
        for ft in range(8):
            pr.t(lambda ft=ft, sub=sub, w=w: nc.tensor.matmul(
                out=PPs[0][0:1, 0:w], lhsT=ones128[:, 0:1],
                rhs=x_sb[:, ft * CH + sub * 512: ft * CH + sub * 512 + w],
                start=(ft == 0), stop=(ft == 7)))
        pr.s(lambda sub=sub, w=w: nc.scalar.activation(
            out=mrow[0:1, sub * 512: sub * 512 + w], in_=PPs[0][0:1, 0:w],
            func=AF.Copy, scale=env['cst'][0:1, 0:1]))
        for ft in range(8):
            pr.s(lambda ft=ft, sub=sub, w=w: nc.scalar.activation(
                out=scr[:, 0:w], in_=x_sb[:, ft * CH + sub * 512: ft * CH + sub * 512 + w],
                func=AF.Square))
            pr.t(lambda ft=ft, w=w: nc.tensor.matmul(
                out=PPs[1][0:1, 0:w], lhsT=ones128[:, 0:1], rhs=scr[:, 0:w],
                start=(ft == 0), stop=(ft == 7)))
        pr.s(lambda sub=sub, w=w: nc.scalar.activation(
            out=vrow[0:1, sub * 512: sub * 512 + w], in_=PPs[1][0:1, 0:w],
            func=AF.Copy, scale=env['cst'][0:1, 0:1]))
    pr.v(lambda: nc.vector.tensor_mul(rrow[0:1, 0:CH], mrow[0:1, 0:CH], mrow[0:1, 0:CH]))
    pr.v(lambda: nc.vector.tensor_sub(vrow[0:1, 0:CH], vrow[0:1, 0:CH], rrow[0:1, 0:CH]))
    pr.s(lambda: nc.scalar.activation(out=rrow[0:1, 0:CH], in_=vrow[0:1, 0:CH],
                                      func=AF.Ln, bias=env['cst'][0:1, 1:2]))
    pr.s(lambda: nc.scalar.activation(out=rrow[0:1, 0:CH], in_=rrow[0:1, 0:CH],
                                      func=AF.Exp, scale=env['cst'][0:1, 2:3]))
    for sub in range(subs):
        w = min(512, CH - sub * 512)
        pr.t(lambda sub=sub, w=w: nc.tensor.matmul(
            out=PPs[0][0:128, 0:w], lhsT=ones1[0:1, 0:128],
            rhs=mrow[0:1, sub * 512: sub * 512 + w], start=True, stop=True))
        pr.t(lambda sub=sub, w=w: nc.tensor.matmul(
            out=PPs[1][0:128, 0:w], lhsT=ones1[0:1, 0:128],
            rhs=rrow[0:1, sub * 512: sub * 512 + w], start=True, stop=True))
        for ft in range(8):
            c0 = ft * CH + sub * 512
            pr.v(lambda c0=c0, w=w: nc.vector.tensor_sub(
                scr[:, 0:w], x_sb[:, c0:c0 + w], PPs[0][0:128, 0:w]))
            pr.v(lambda c0=c0, w=w: nc.vector.tensor_tensor(
                scr[:, 0:w], scr[:, 0:w], PPs[1][0:128, 0:w], op=OP.mult))
            if gcol is not None:
                pr.v(lambda ft=ft, c0=c0, w=w: nc.vector.tensor_scalar(
                    out=scr[:, 0:w], in0=scr[:, 0:w],
                    scalar1=gcol[:, ft:ft + 1], scalar2=bcol[:, ft:ft + 1],
                    op0=OP.mult, op1=OP.add))
            if xo_f32 is not None:
                pr.v(lambda c0=c0, w=w, xo=xo_f32: nc.vector.tensor_copy(
                    out=xo[:, c0:c0 + w], in_=scr[:, 0:w]))
            if xo_bf is not None:
                pr.v(lambda c0=c0, w=w, xo=xo_bf: nc.vector.tensor_copy(
                    out=xo[:, c0:c0 + w], in_=scr[:, 0:w]))


def emit_top2(pr, nc, env, lg_sb, E, CH, renorm, comb_fm_bf):
    PPs = env['PPs']
    eye = env['eye']
    tb = env['tb']
    tc1 = env['tc1']
    cA = env['cA']
    cB = env['cB']
    cC = env['cC']
    for blk in range(CH // 128):
        pr.t(lambda blk=blk: nc.tensor.transpose(
            out=PPs[2][0:128, 0:E], in_=lg_sb[0:E, blk * 128:(blk + 1) * 128],
            identity=eye[0:E, 0:E]))
        pr.v(lambda E=E: nc.vector.tensor_copy(out=tb[:, 0:E], in_=PPs[2][0:128, 0:E]))
        pr.v(lambda E=E: nc.vector.reduce_max(cA[:, 0:1], tb[:, 0:E], axis=AX.X, negate=True))
        pr.s(lambda E=E: nc.scalar.activation(out=tb[:, 0:E], in_=tb[:, 0:E],
                                              func=AF.Exp, bias=cA[:, 0:1]))
        pr.v(lambda E=E: nc.vector.reduce_sum(cA[:, 0:1], tb[:, 0:E], axis=AX.X))
        pr.v(lambda: nc.vector.reciprocal(cB[:, 0:1], cA[:, 0:1]))
        pr.v(lambda E=E: nc.vector.tensor_scalar_mul(tb[:, 0:E], tb[:, 0:E], cB[:, 0:1]))
        pr.v(lambda E=E: nc.vector.reduce_max(cA[:, 0:1], tb[:, 0:E], axis=AX.X))
        pr.v(lambda E=E: nc.vector.tensor_scalar(
            out=tc1[:, 0:E], in0=tb[:, 0:E], scalar1=cA[:, 0:1], scalar2=None,
            op0=OP.is_equal))
        pr.v(lambda E=E: nc.vector.tensor_sub(tc1[:, 0:E], tb[:, 0:E], tc1[:, 0:E]))
        pr.v(lambda E=E: nc.vector.reduce_max(cB[:, 0:1], tc1[:, 0:E], axis=AX.X))
        pr.v(lambda E=E: nc.vector.tensor_scalar(
            out=tc1[:, 0:E], in0=tb[:, 0:E], scalar1=cB[:, 0:1], scalar2=None,
            op0=OP.is_ge))
        pr.v(lambda E=E: nc.vector.tensor_mul(tb[:, 0:E], tb[:, 0:E], tc1[:, 0:E]))
        if renorm:
            pr.v(lambda: nc.vector.tensor_add(cC[:, 0:1], cA[:, 0:1], cB[:, 0:1]))
            pr.v(lambda: nc.vector.tensor_scalar_add(cC[:, 0:1], cC[:, 0:1], 1e-9))
            pr.v(lambda: nc.vector.reciprocal(cC[:, 0:1], cC[:, 0:1]))
            pr.v(lambda E=E: nc.vector.tensor_scalar_mul(tb[:, 0:E], tb[:, 0:E], cC[:, 0:1]))
        pr.t(lambda E=E: nc.tensor.transpose(
            out=PPs[2][0:E, 0:128], in_=tb[0:128, 0:E], identity=eye[:, :]))
        pr.v(lambda blk=blk, E=E: nc.vector.tensor_copy(
            out=comb_fm_bf[0:E, blk * 128:(blk + 1) * 128], in_=PPs[2][0:E, 0:128]))


def common_env(nc, es, prog, wcols=8192):
    env = {}
    env['wst'] = es.enter_context(nc.sbuf_tensor("wst", [128, wcols], F32))
    env['wbf'] = es.enter_context(nc.sbuf_tensor("wbf", [128, wcols], BF16))
    env['PPs'] = [es.enter_context(nc.psum_tensor("pp%d" % i, [128, 512], F32))
                  for i in range(8)]
    env['ones128'] = es.enter_context(nc.sbuf_tensor("o128", [128, 1], F32))
    env['ones1'] = es.enter_context(nc.sbuf_tensor("o1", [1, 128], F32))
    env['mrow'] = es.enter_context(nc.sbuf_tensor("mrow", [1, 1024], F32))
    env['vrow'] = es.enter_context(nc.sbuf_tensor("vrow", [1, 1024], F32))
    env['rrow'] = es.enter_context(nc.sbuf_tensor("rrow", [1, 1024], F32))
    env['scr'] = es.enter_context(nc.sbuf_tensor("scr", [128, 512], F32))
    cst = es.enter_context(nc.sbuf_tensor("cst", [128, 4], F32))
    prog.v(lambda: nc.vector.memset(env['ones128'][:, :], 1.0))
    prog.v(lambda: nc.vector.memset(env['ones1'][:, :], 1.0))
    prog.v(lambda: nc.vector.memset(cst[:, 0:1], 1.0 / DM))
    prog.v(lambda: nc.vector.memset(cst[:, 1:2], LNEPS))
    prog.v(lambda: nc.vector.memset(cst[:, 2:3], -0.5))
    prog.v(lambda: nc.vector.memset(cst[:, 3:4], -1.0))
    env['cst'] = cst
    return env


def hin(nc, name, shape, dt=F32):
    return nc.dram_tensor(name, shape, dt, kind="ExternalInput").ap()


def hout(nc, name, shape, dt=F32):
    return nc.dram_tensor(name, shape, dt, kind="ExternalOutput").ap()


def build_ffn(kind):
    nc = bass.Bass("TRN2", target_bir_lowering=False)
    CH = 1024
    x_in = hin(nc, "x", [DM, TC])
    if kind == 'ffn0':
        E = 7
        MF = 4096
        sgT = hin(nc, "sgT", [DM, MF])
        suT = hin(nc, "suT", [DM, MF])
        sdT = hin(nc, "sdT", [MF, DM])
        rgT = hin(nc, "rgT", [E, DM, MF])
        ruT = hin(nc, "ruT", [E, DM, MF])
        rdT = hin(nc, "rdT", [E, MF, DM])
        renorm = True
    else:
        E = 8
        MF = 2048
        rgT = hin(nc, "rgT", [E, DM, MF])
        ruT = hin(nc, "ruT", [E, DM, MF])
        rdT = hin(nc, "rdT", [E, MF, DM])
        renorm = False
    rwT = hin(nc, "rwT", [DM, E])
    selE = hin(nc, "selE", [E, E, 128], BF16)
    eye = hin(nc, "eye", [128, 128])
    y_out = hout(nc, "y", [DM, TC])

    es = ExitStack()
    pr = P(nc)
    env = common_env(nc, es, pr, wcols=4096)
    env['eye'] = es.enter_context(nc.sbuf_tensor("eye_s", [128, 128], F32))
    pr.d(lambda: nc.sync.dma_start(out=env['eye'][:, :], in_=eye[:, :]))
    x_sb = es.enter_context(nc.sbuf_tensor("x_sb", [128, 8 * CH], F32))
    xn_bf = es.enter_context(nc.sbuf_tensor("xn_bf", [128, 8 * CH], BF16))
    h_bf = es.enter_context(nc.sbuf_tensor("h_bf", [128, (MF // 128) * CH], BF16))
    acc = es.enter_context(nc.sbuf_tensor("acc", [128, 8 * CH], F32))
    lg_sb = es.enter_context(nc.sbuf_tensor("lg_sb", [E, CH], F32))
    comb = es.enter_context(nc.sbuf_tensor("comb", [E, CH], BF16))
    cb_sb = es.enter_context(nc.sbuf_tensor("cb_sb", [128, CH], BF16))
    sel_sb = es.enter_context(nc.sbuf_tensor("sel_sb", [E, E * 128], BF16))
    env['tb'] = es.enter_context(nc.sbuf_tensor("tb", [128, 64], F32))
    env['tc1'] = es.enter_context(nc.sbuf_tensor("tc1", [128, 64], F32))
    env['cA'] = es.enter_context(nc.sbuf_tensor("cA", [128, 1], F32))
    env['cB'] = es.enter_context(nc.sbuf_tensor("cB", [128, 1], F32))
    env['cC'] = es.enter_context(nc.sbuf_tensor("cC", [128, 1], F32))
    sil = es.enter_context(nc.sbuf_tensor("sil", [128, 512], F32))
    o_sb = es.enter_context(nc.sbuf_tensor("o_sb", [128, 512], F32))
    for e in range(E):
        pr.d(lambda e=e: nc.sync.dma_start(out=sel_sb[:, e * 128:(e + 1) * 128],
                                           in_=selE[e]))

    rhs_x = lambda kb, sub: xn_bf[:, kb * CH + sub * 512: kb * CH + sub * 512 + 512]
    rhs_h = lambda kb, sub: h_bf[:, kb * CH + sub * 512: kb * CH + sub * 512 + 512]
    nch = TC // CH
    for ci in range(nch):
        t0 = ci * CH
        for ft in range(8):
            pr.d(lambda ft=ft, t0=t0: nc.sync.dma_start(
                out=x_sb[:, ft * CH:(ft + 1) * CH],
                in_=x_in[ft * 128:(ft + 1) * 128, t0:t0 + CH]))
        emit_ln(pr, nc, env, x_sb, CH, xo_bf=xn_bf)
        def lg_out(mt, sub, ps):
            pr.v(lambda sub=sub, ps=ps: nc.vector.tensor_copy(
                out=lg_sb[0:E, sub * 512:(sub + 1) * 512], in_=ps))
        emit_mm(pr, nc, env, rwT, DM, E, CH, rhs_x, lg_out)
        emit_top2(pr, nc, env, lg_sb, E, CH, renorm, comb)

        def do_glu(gT, uT):
            mts = MF // 128
            for g0 in range(0, mts, 2):
                emit_mm(pr, nc, env, gT, DM, MF, CH, rhs_x, lambda *a: None,
                        mlist=[g0, g0 + 1], bank0=0, nbank=4)

                def up_out(mt, sub, ps, g0=g0):
                    bank_g = (mt - g0) * 2 + sub
                    pr.s(lambda ps=ps, bank_g=bank_g: nc.scalar.activation(
                        out=sil[:, 0:512], in_=env['PPs'][bank_g][0:128, 0:512],
                        func=AF.Silu))
                    pr.v(lambda mt=mt, sub=sub, ps=ps: nc.vector.tensor_tensor(
                        out=h_bf[:, mt * CH + sub * 512: mt * CH + sub * 512 + 512],
                        in0=sil[:, 0:512], in1=ps, op=OP.mult))
                emit_mm(pr, nc, env, uT, DM, MF, CH, rhs_x, up_out,
                        mlist=[g0, g0 + 1], bank0=4, nbank=4)

        if kind == 'ffn0':
            do_glu(sgT, suT)

            def dn_out(mt, sub, ps):
                pr.v(lambda mt=mt, sub=sub, ps=ps: nc.vector.tensor_copy(
                    out=acc[:, mt * CH + sub * 512: mt * CH + sub * 512 + 512], in_=ps))
            emit_mm(pr, nc, env, sdT, MF, DM, CH, rhs_h, dn_out)
            first_e = False
        else:
            first_e = True
        for e in range(E):
            for sub in range(CH // 512):
                pr.t(lambda e=e, sub=sub: nc.tensor.matmul(
                    out=env['PPs'][0][0:128, 0:512],
                    lhsT=sel_sb[:, e * 128:(e + 1) * 128],
                    rhs=comb[0:E, sub * 512:(sub + 1) * 512], start=True, stop=True))
                pr.v(lambda sub=sub: nc.vector.tensor_copy(
                    out=cb_sb[:, sub * 512:(sub + 1) * 512],
                    in_=env['PPs'][0][0:128, 0:512]))
            do_glu(rgT[e], ruT[e])

            def rdn_out(mt, sub, ps, first=(first_e and e == 0)):
                c0 = mt * CH + sub * 512
                if first:
                    pr.v(lambda c0=c0, sub=sub, ps=ps: nc.vector.tensor_tensor(
                        out=acc[:, c0:c0 + 512], in0=ps,
                        in1=cb_sb[:, sub * 512:sub * 512 + 512], op=OP.mult))
                else:
                    pr.v(lambda sub=sub, ps=ps: nc.vector.tensor_tensor(
                        out=o_sb[:, 0:512], in0=ps,
                        in1=cb_sb[:, sub * 512:sub * 512 + 512], op=OP.mult))
                    pr.v(lambda c0=c0: nc.vector.tensor_add(
                        acc[:, c0:c0 + 512], acc[:, c0:c0 + 512], o_sb[:, 0:512]))
            emit_mm(pr, nc, env, rdT[e], MF, DM, CH, rhs_h, rdn_out)
        for ft in range(8):
            for sub in range(CH // 512):
                c0 = ft * CH + sub * 512
                pr.v(lambda c0=c0: nc.vector.scalar_tensor_tensor(
                    out=x_sb[:, c0:c0 + 512], in0=acc[:, c0:c0 + 512], scalar=0.5,
                    in1=x_sb[:, c0:c0 + 512], op0=OP.mult, op1=OP.add))
            pr.d(lambda ft=ft, t0=t0: nc.sync.dma_start(
                out=y_out[ft * 128:(ft + 1) * 128, t0:t0 + CH],
                in_=x_sb[:, ft * CH:(ft + 1) * CH]))
    finish(nc, pr)
    es.close()
    return nc


def build_rwkvA():
    nc = bass.Bass("TRN2", target_bir_lowering=False)
    CH = 512
    x_in = hin(nc, "x", [DM, TC])
    xprev_in = hin(nc, "xprev", [128, 8])
    fmask = hin(nc, "fmask", [128, 1])
    ln0g = hin(nc, "ln0g", [128, 8])
    ln0b = hin(nc, "ln0b", [128, 8])
    preg = hin(nc, "preg", [128, 8])
    preb = hin(nc, "preb", [128, 8])
    mixk = hin(nc, "mixk", [128, 8])
    mixv = hin(nc, "mixv", [128, 8])
    mixr = hin(nc, "mixr", [128, 8])
    Wk = hin(nc, "Wk", [DM, HU])
    Wv = hin(nc, "Wv", [DM, HU])
    Wr = hin(nc, "Wr", [DM, HU])
    kT_o = hout(nc, "kT", [HU, TC])
    vT_o = hout(nc, "vT", [HU, TC])
    srT_o = hout(nc, "srT", [HU, TC])
    t0_o = hout(nc, "t0", [DM, TC])

    es = ExitStack()
    pr = P(nc)
    env = common_env(nc, es, pr)
    x_sb = es.enter_context(nc.sbuf_tensor("x_sb", [128, 8 * CH], F32))
    t_sb = es.enter_context(nc.sbuf_tensor("t_sb", [128, 8 * CH], F32))
    t0_sb = es.enter_context(nc.sbuf_tensor("t0_sb", [128, 8 * CH], F32))
    xn_sb = es.enter_context(nc.sbuf_tensor("xn_sb", [128, 8 * CH], F32))
    xx_sb = es.enter_context(nc.sbuf_tensor("xx_sb", [128, 8 * CH], F32))
    xm_bf = [es.enter_context(nc.sbuf_tensor("xm%d" % i, [128, 8 * CH], BF16))
             for i in range(3)]
    par = {}
    for nm, ap in [('ln0g', ln0g), ('ln0b', ln0b), ('preg', preg), ('preb', preb),
                   ('mixk', mixk), ('mixv', mixv), ('mixr', mixr), ('xprev', xprev_in),
                   ('fmask', fmask)]:
        tlw = 1 if nm == 'fmask' else 8
        par[nm] = es.enter_context(nc.sbuf_tensor(nm + "_s", [128, tlw], F32))
        pr.d(lambda nm=nm, ap=ap: nc.sync.dma_start(out=par[nm][:, :], in_=ap[:, :]))
    xpcol = es.enter_context(nc.sbuf_tensor("xpcol", [128, 8], F32))
    ko = es.enter_context(nc.sbuf_tensor("ko", [128, 512], F32))
    emit_ln(pr, nc, env, par['xprev'], 1, xo_f32=par['xprev'])
    emit_ln(pr, nc, env, par['xprev'], 1, xo_f32=par['xprev'],
            gcol=par['ln0g'], bcol=par['ln0b'])
    emit_ln(pr, nc, env, par['xprev'], 1, xo_f32=xpcol,
            gcol=par['preg'], bcol=par['preb'])
    for ft in range(8):
        pr.v(lambda ft=ft: nc.vector.tensor_scalar(
            out=xpcol[:, ft:ft + 1], in0=xpcol[:, ft:ft + 1],
            scalar1=par['fmask'][:, 0:1], scalar2=None, op0=OP.mult))

    mixes = [('mixk', 0), ('mixv', 1), ('mixr', 2)]
    nch = TC // CH
    for ci in range(nch):
        t0 = ci * CH
        for ft in range(8):
            pr.d(lambda ft=ft, t0=t0: nc.sync.dma_start(
                out=x_sb[:, ft * CH:(ft + 1) * CH],
                in_=x_in[ft * 128:(ft + 1) * 128, t0:t0 + CH]))
        emit_ln(pr, nc, env, x_sb, CH, xo_f32=t_sb)
        emit_ln(pr, nc, env, t_sb, CH, xo_f32=t0_sb, gcol=par['ln0g'], bcol=par['ln0b'])
        for ft in range(8):
            pr.d(lambda ft=ft, t0=t0: nc.sync.dma_start(
                out=t0_o[ft * 128:(ft + 1) * 128, t0:t0 + CH],
                in_=t0_sb[:, ft * CH:(ft + 1) * CH]))
        emit_ln(pr, nc, env, t0_sb, CH, xo_f32=xn_sb, gcol=par['preg'], bcol=par['preb'])
        for ft in range(8):
            pr.v(lambda ft=ft: nc.vector.tensor_copy(
                out=xx_sb[:, ft * CH + 1: ft * CH + CH],
                in_=xn_sb[:, ft * CH: ft * CH + CH - 1]))
            pr.v(lambda ft=ft: nc.vector.tensor_copy(
                out=xx_sb[:, ft * CH: ft * CH + 1], in_=xpcol[:, ft:ft + 1]))
            pr.v(lambda ft=ft: nc.vector.tensor_copy(
                out=xpcol[:, ft:ft + 1], in_=xn_sb[:, ft * CH + CH - 1: ft * CH + CH]))
        for nm, ix in mixes:
            for ft in range(8):
                c0 = ft * CH
                pr.v(lambda c0=c0: nc.vector.tensor_sub(
                    env['scr'][:, 0:CH], xn_sb[:, c0:c0 + CH], xx_sb[:, c0:c0 + CH]))
                pr.v(lambda nm=nm, ft=ft, c0=c0: nc.vector.scalar_tensor_tensor(
                    out=env['scr'][:, 0:CH], in0=env['scr'][:, 0:CH],
                    scalar=par[nm][:, ft:ft + 1], in1=xx_sb[:, c0:c0 + CH],
                    op0=OP.mult, op1=OP.add))
                pr.v(lambda ix=ix, c0=c0: nc.vector.tensor_copy(
                    out=xm_bf[ix][:, c0:c0 + CH], in_=env['scr'][:, 0:CH]))
        def mk_rhs(ix):
            return lambda kb, sub: xm_bf[ix][:, kb * CH + sub * 512: kb * CH + sub * 512 + 512]

        def k_out(mt, sub, ps):
            pr.v(lambda ps=ps: nc.vector.tensor_scalar_min(ko[:, 0:512], ps, 60.0))
            pr.s(lambda: nc.scalar.activation(out=ko[:, 0:512], in_=ko[:, 0:512],
                                              func=AF.Exp))
            pr.d(lambda mt=mt, t0=t0: nc.sync.dma_start(
                out=kT_o[mt * 128:(mt + 1) * 128, t0:t0 + CH], in_=ko[:, 0:512]))
        emit_mm(pr, nc, env, Wk, DM, HU, CH, mk_rhs(0), k_out)

        def v_out(mt, sub, ps):
            pr.v(lambda ps=ps: nc.vector.tensor_copy(out=ko[:, 0:512], in_=ps))
            pr.d(lambda mt=mt, t0=t0: nc.sync.dma_start(
                out=vT_o[mt * 128:(mt + 1) * 128, t0:t0 + CH], in_=ko[:, 0:512]))
        emit_mm(pr, nc, env, Wv, DM, HU, CH, mk_rhs(1), v_out)

        def r_out(mt, sub, ps):
            pr.s(lambda ps=ps: nc.scalar.activation(out=ko[:, 0:512], in_=ps,
                                                    func=AF.Sigmoid))
            pr.d(lambda mt=mt, t0=t0: nc.sync.dma_start(
                out=srT_o[mt * 128:(mt + 1) * 128, t0:t0 + CH], in_=ko[:, 0:512]))
        emit_mm(pr, nc, env, Wr, DM, HU, CH, mk_rhs(2), r_out)
    finish(nc, pr)
    es.close()
    return nc


def build_scan():
    nc = bass.Bass("TRN2", target_bir_lowering=False)
    NCH = HU // NCORE
    T = 8192
    k_in = hin(nc, "k", [NCH, TT])
    v_in = hin(nc, "v", [NCH, TT])
    dec_in = hin(nc, "dec", [128, 4])
    fir_in = hin(nc, "fir", [128, 4])
    rat_o = hout(nc, "rat", [NCH, TT])
    es = ExitStack()
    pr = P(nc)
    kt = es.enter_context(nc.sbuf_tensor("kt", [128, T], F32))
    vt = es.enter_context(nc.sbuf_tensor("vt", [128, T], F32))
    sk = es.enter_context(nc.sbuf_tensor("sk", [128, T], F32))
    skv = es.enter_context(nc.sbuf_tensor("skv", [128, T], F32))
    dcol = es.enter_context(nc.sbuf_tensor("dcol", [128, 4], F32))
    fcol = es.enter_context(nc.sbuf_tensor("fcol", [128, 4], F32))
    db = es.enter_context(nc.sbuf_tensor("db", [128, 2048], F32))
    pr.d(lambda: nc.sync.dma_start(out=dcol[:, :], in_=dec_in[:, :]))
    pr.d(lambda: nc.sync.dma_start(out=fcol[:, :], in_=fir_in[:, :]))
    pr.s(lambda: nc.scalar.activation(out=dcol[:, :], in_=dcol[:, :], func=AF.Exp))
    pr.v(lambda: nc.vector.tensor_scalar_mul(dcol[:, :], dcol[:, :], -1.0))
    pr.s(lambda: nc.scalar.activation(out=dcol[:, :], in_=dcol[:, :], func=AF.Exp))
    pr.s(lambda: nc.scalar.activation(out=fcol[:, :], in_=fcol[:, :], func=AF.Exp))
    for ct in range(4):
        pr.v(lambda: nc.vector.memset(db[:, :], 1.0))
        pr.v(lambda ct=ct: nc.vector.tensor_scalar_mul(db[:, :], db[:, :],
                                                       dcol[:, ct:ct + 1]))
        for b in range(2):
            pr.d(lambda ct=ct, b=b: nc.sync.dma_start(
                out=kt[:, :], in_=k_in[ct * 128:(ct + 1) * 128, b * T:(b + 1) * T]))
            pr.d(lambda ct=ct, b=b: nc.sync.dma_start(
                out=vt[:, :], in_=v_in[ct * 128:(ct + 1) * 128, b * T:(b + 1) * T]))
            pr.v(lambda: nc.vector.tensor_mul(vt[:, :], vt[:, :], kt[:, :]))
            for tens, st in ((kt, sk), (vt, skv)):
                for c in range(4):
                    c0 = c * 2048
                    init = 0.0 if c == 0 else st[:, c0 - 1:c0]
                    pr.v(lambda tens=tens, st=st, c0=c0, init=init:
                         nc.vector.tensor_tensor_scan(
                             out=st[:, c0:c0 + 2048], data0=db[:, :],
                             data1=tens[:, c0:c0 + 2048], initial=init,
                             op0=OP.mult, op1=OP.add))
            for tens, st in ((kt, sk), (vt, skv)):
                pr.v(lambda tens=tens, ct=ct: nc.vector.tensor_scalar_mul(
                    tens[:, 0:1], tens[:, 0:1], fcol[:, ct:ct + 1]))
                pr.v(lambda tens=tens, st=st, ct=ct: nc.vector.scalar_tensor_tensor(
                    out=tens[:, 1:T], in0=tens[:, 1:T], scalar=fcol[:, ct:ct + 1],
                    in1=st[:, 0:T - 1], op0=OP.mult, op1=OP.add))
            pr.v(lambda: nc.vector.tensor_scalar_add(kt[:, :], kt[:, :], 1e-8))
            pr.v(lambda: nc.vector.reciprocal(kt[:, :], kt[:, :]))
            pr.v(lambda: nc.vector.tensor_mul(vt[:, :], vt[:, :], kt[:, :]))
            pr.d(lambda ct=ct, b=b: nc.sync.dma_start(
                out=rat_o[ct * 128:(ct + 1) * 128, b * T:(b + 1) * T], in_=vt[:, :]))
    finish(nc, pr)
    es.close()
    return nc


def build_rwkvB():
    nc = bass.Bass("TRN2", target_bir_lowering=False)
    CH = 512
    rat_in = hin(nc, "rat", [HU, TC])
    sr_in = hin(nc, "sr", [HU, TC])
    t0_in = hin(nc, "t0", [DM, TC])
    Wo = hin(nc, "Wo", [HU, DM])
    t1_o = hout(nc, "t1", [DM, TC])
    es = ExitStack()
    pr = P(nc)
    env = common_env(nc, es, pr)
    rw_bf = es.enter_context(nc.sbuf_tensor("rw_bf", [128, 32 * CH], BF16))
    t0_sb = es.enter_context(nc.sbuf_tensor("t0_sb", [128, 8 * CH], F32))
    a_sb = es.enter_context(nc.sbuf_tensor("a_sb", [128, CH], F32))
    b_sb = es.enter_context(nc.sbuf_tensor("b_sb", [128, CH], F32))
    o_sb = es.enter_context(nc.sbuf_tensor("oo_sb", [128, 512], F32))
    for ci in range(TC // CH):
        t0 = ci * CH
        for kb in range(32):
            pr.d(lambda kb=kb, t0=t0: nc.sync.dma_start(
                out=a_sb[:, :], in_=rat_in[kb * 128:(kb + 1) * 128, t0:t0 + CH]))
            pr.d(lambda kb=kb, t0=t0: nc.sync.dma_start(
                out=b_sb[:, :], in_=sr_in[kb * 128:(kb + 1) * 128, t0:t0 + CH]))
            pr.v(lambda: nc.vector.tensor_mul(a_sb[:, :], a_sb[:, :], b_sb[:, :]))
            pr.v(lambda kb=kb: nc.vector.tensor_copy(
                out=rw_bf[:, kb * CH:(kb + 1) * CH], in_=a_sb[:, :]))
        for ft in range(8):
            pr.d(lambda ft=ft, t0=t0: nc.sync.dma_start(
                out=t0_sb[:, ft * CH:(ft + 1) * CH],
                in_=t0_in[ft * 128:(ft + 1) * 128, t0:t0 + CH]))

        def wo_out(mt, sub, ps):
            pr.v(lambda mt=mt, ps=ps: nc.vector.tensor_tensor(
                out=o_sb[:, 0:512], in0=ps, in1=t0_sb[:, mt * CH:mt * CH + 512],
                op=OP.add))
            pr.d(lambda mt=mt, t0=t0: nc.sync.dma_start(
                out=t1_o[mt * 128:(mt + 1) * 128, t0:t0 + CH], in_=o_sb[:, 0:512]))
        emit_mm(pr, nc, env, Wo, HU, DM, CH,
                lambda kb, sub: rw_bf[:, kb * CH: kb * CH + 512], wo_out)
    finish(nc, pr)
    es.close()
    return nc


def build_chanmix():
    nc = bass.Bass("TRN2", target_bir_lowering=False)
    CH = 512
    t1_in = hin(nc, "t1", [DM, TC])
    x1_in = hin(nc, "x1", [DM, TC])
    tprev_in = hin(nc, "tprev", [128, 8])
    fmask = hin(nc, "fmask", [128, 1])
    postg = hin(nc, "postg", [128, 8])
    postb = hin(nc, "postb", [128, 8])
    cmixk = hin(nc, "cmixk", [128, 8])
    cmixr = hin(nc, "cmixr", [128, 8])
    cWk = hin(nc, "cWk", [DM, HU])
    cWv = hin(nc, "cWv", [HU, DM])
    cWr = hin(nc, "cWr", [DM, DM])
    x3_o = hout(nc, "x3", [DM, TC])
    es = ExitStack()
    pr = P(nc)
    env = common_env(nc, es, pr)
    t1_sb = es.enter_context(nc.sbuf_tensor("t1_sb", [128, 8 * CH], F32))
    x1_sb = es.enter_context(nc.sbuf_tensor("x1_sb", [128, 8 * CH], F32))
    xn_sb = es.enter_context(nc.sbuf_tensor("xn_sb", [128, 8 * CH], F32))
    xx_sb = es.enter_context(nc.sbuf_tensor("xx_sb", [128, 8 * CH], F32))
    xk_bf = es.enter_context(nc.sbuf_tensor("xk_bf", [128, 8 * CH], BF16))
    xr_bf = es.enter_context(nc.sbuf_tensor("xr_bf", [128, 8 * CH], BF16))
    hk_bf = es.enter_context(nc.sbuf_tensor("hk_bf", [128, 32 * CH], BF16))
    sg_sb = es.enter_context(nc.sbuf_tensor("sg_sb", [128, 8 * CH], F32))
    o_sb = es.enter_context(nc.sbuf_tensor("oo_sb", [128, 512], F32))
    par = {}
    for nm, ap in [('postg', postg), ('postb', postb), ('cmixk', cmixk),
                   ('cmixr', cmixr), ('tprev', tprev_in), ('fmask', fmask)]:
        tlw = 1 if nm == 'fmask' else 8
        par[nm] = es.enter_context(nc.sbuf_tensor(nm + "_s", [128, tlw], F32))
        pr.d(lambda nm=nm, ap=ap: nc.sync.dma_start(out=par[nm][:, :], in_=ap[:, :]))
    xpcol = es.enter_context(nc.sbuf_tensor("xpcol", [128, 8], F32))
    emit_ln(pr, nc, env, par['tprev'], 1, xo_f32=xpcol,
            gcol=par['postg'], bcol=par['postb'])
    for ft in range(8):
        pr.v(lambda ft=ft: nc.vector.tensor_scalar(
            out=xpcol[:, ft:ft + 1], in0=xpcol[:, ft:ft + 1],
            scalar1=par['fmask'][:, 0:1], scalar2=None, op0=OP.mult))
    for ci in range(TC // CH):
        t0 = ci * CH
        for ft in range(8):
            pr.d(lambda ft=ft, t0=t0: nc.sync.dma_start(
                out=t1_sb[:, ft * CH:(ft + 1) * CH],
                in_=t1_in[ft * 128:(ft + 1) * 128, t0:t0 + CH]))
            pr.d(lambda ft=ft, t0=t0: nc.sync.dma_start(
                out=x1_sb[:, ft * CH:(ft + 1) * CH],
                in_=x1_in[ft * 128:(ft + 1) * 128, t0:t0 + CH]))
        emit_ln(pr, nc, env, t1_sb, CH, xo_f32=xn_sb,
                gcol=par['postg'], bcol=par['postb'])
        for ft in range(8):
            pr.v(lambda ft=ft: nc.vector.tensor_copy(
                out=xx_sb[:, ft * CH + 1: ft * CH + CH],
                in_=xn_sb[:, ft * CH: ft * CH + CH - 1]))
            pr.v(lambda ft=ft: nc.vector.tensor_copy(
                out=xx_sb[:, ft * CH: ft * CH + 1], in_=xpcol[:, ft:ft + 1]))
            pr.v(lambda ft=ft: nc.vector.tensor_copy(
                out=xpcol[:, ft:ft + 1], in_=xn_sb[:, ft * CH + CH - 1: ft * CH + CH]))
        for nm, dst in (('cmixk', xk_bf), ('cmixr', xr_bf)):
            for ft in range(8):
                c0 = ft * CH
                pr.v(lambda c0=c0: nc.vector.tensor_sub(
                    env['scr'][:, 0:CH], xn_sb[:, c0:c0 + CH], xx_sb[:, c0:c0 + CH]))
                pr.v(lambda nm=nm, ft=ft, c0=c0: nc.vector.scalar_tensor_tensor(
                    out=env['scr'][:, 0:CH], in0=env['scr'][:, 0:CH],
                    scalar=par[nm][:, ft:ft + 1], in1=xx_sb[:, c0:c0 + CH],
                    op0=OP.mult, op1=OP.add))
                pr.v(lambda dst=dst, c0=c0: nc.vector.tensor_copy(
                    out=dst[:, c0:c0 + CH], in_=env['scr'][:, 0:CH]))

        def hk_out(mt, sub, ps):
            pr.s(lambda mt=mt, ps=ps: nc.scalar.activation(
                out=hk_bf[:, mt * CH:mt * CH + 512], in_=ps, func=AF.Silu))
        emit_mm(pr, nc, env, cWk, DM, HU, CH,
                lambda kb, sub: xk_bf[:, kb * CH:kb * CH + 512], hk_out)

        def sg_out(mt, sub, ps):
            pr.s(lambda mt=mt, ps=ps: nc.scalar.activation(
                out=sg_sb[:, mt * CH:mt * CH + 512], in_=ps, func=AF.Sigmoid))
        emit_mm(pr, nc, env, cWr, DM, DM, CH,
                lambda kb, sub: xr_bf[:, kb * CH:kb * CH + 512], sg_out)

        def kv_out(mt, sub, ps):
            pr.v(lambda mt=mt, ps=ps: nc.vector.tensor_tensor(
                out=o_sb[:, 0:512], in0=ps, in1=sg_sb[:, mt * CH:mt * CH + 512],
                op=OP.add))
            pr.v(lambda mt=mt: nc.vector.tensor_add(
                o_sb[:, 0:512], o_sb[:, 0:512], t1_sb[:, mt * CH:mt * CH + 512]))
            pr.v(lambda mt=mt: nc.vector.tensor_add(
                o_sb[:, 0:512], o_sb[:, 0:512], x1_sb[:, mt * CH:mt * CH + 512]))
            pr.d(lambda mt=mt, t0=t0: nc.sync.dma_start(
                out=x3_o[mt * 128:(mt + 1) * 128, t0:t0 + CH], in_=o_sb[:, 0:512]))
        emit_mm(pr, nc, env, cWv, HU, DM, CH,
                lambda kb, sub: hk_bf[:, kb * CH:kb * CH + 512], kv_out)
    finish(nc, pr)
    es.close()
    return nc


def build_qkvg():
    nc = bass.Bass("TRN2", target_bir_lowering=False)
    CH = 512
    x_in = hin(nc, "x", [DM, TC])
    Wq = hin(nc, "Wq", [DM, DM])
    Wk = hin(nc, "Wk", [DM, DM])
    Wv = hin(nc, "Wv", [DM, DM])
    Wg = hin(nc, "Wg", [DM, DM])
    Wa = hin(nc, "Wa", [DM, 16])
    ba = hin(nc, "ba", [16, 1])
    q_o = hout(nc, "q", [DM, TC])
    k_o = hout(nc, "k", [DM, TC])
    v_o = hout(nc, "v", [DM, TC])
    g_o = hout(nc, "g", [DM, TC])
    la_o = hout(nc, "la", [16, TC])
    es = ExitStack()
    pr = P(nc)
    env = common_env(nc, es, pr)
    x_sb = es.enter_context(nc.sbuf_tensor("x_sb", [128, 8 * CH], F32))
    xn_bf = es.enter_context(nc.sbuf_tensor("xn_bf", [128, 8 * CH], BF16))
    ko = es.enter_context(nc.sbuf_tensor("ko", [128, 512], F32))
    k2 = es.enter_context(nc.sbuf_tensor("k2", [128, 512], F32))
    bacol = es.enter_context(nc.sbuf_tensor("bacol", [16, 1], F32))
    pr.d(lambda: nc.sync.dma_start(out=bacol[:, :], in_=ba[:, :]))
    rhs = lambda kb, sub: xn_bf[:, kb * CH:kb * CH + 512]
    for ci in range(TC // CH):
        t0 = ci * CH
        for ft in range(8):
            pr.d(lambda ft=ft, t0=t0: nc.sync.dma_start(
                out=x_sb[:, ft * CH:(ft + 1) * CH],
                in_=x_in[ft * 128:(ft + 1) * 128, t0:t0 + CH]))
        emit_ln(pr, nc, env, x_sb, CH, xo_bf=xn_bf)
        for W, out_h, act in ((Wq, q_o, AF.Silu), (Wk, k_o, AF.Silu),
                              (Wv, v_o, None), (Wg, g_o, AF.Sigmoid)):
            def p_out(mt, sub, ps, out_h=out_h, act=act):
                if act is None:
                    pr.v(lambda ps=ps: nc.vector.tensor_copy(out=ko[:, 0:512], in_=ps))
                else:
                    pr.s(lambda ps=ps, act=act: nc.scalar.activation(
                        out=ko[:, 0:512], in_=ps, func=act))
                pr.d(lambda mt=mt, t0=t0, out_h=out_h: nc.sync.dma_start(
                    out=out_h[mt * 128:(mt + 1) * 128, t0:t0 + CH], in_=ko[:, 0:512]))
            emit_mm(pr, nc, env, W, DM, DM, CH, rhs, p_out)

        def a_out(mt, sub, ps):
            pr.v(lambda ps=ps: nc.vector.tensor_scalar(
                out=ko[0:16, 0:512], in0=ps, scalar1=bacol[:, 0:1], scalar2=None,
                op0=OP.add))
            pr.s(lambda: nc.scalar.activation(out=k2[0:16, 0:512], in_=ko[0:16, 0:512],
                                              func=AF.Abs))
            pr.s(lambda: nc.scalar.activation(out=k2[0:16, 0:512], in_=k2[0:16, 0:512],
                                              func=AF.Exp, scale=env['cst'][0:16, 3:4]))
            pr.s(lambda: nc.scalar.activation(out=k2[0:16, 0:512], in_=k2[0:16, 0:512],
                                              func=AF.Ln, bias=env['ones128'][0:16, 0:1]))
            pr.v(lambda: nc.vector.tensor_scalar_max(ko[0:16, 0:512], ko[0:16, 0:512],
                                                     0.0))
            pr.v(lambda: nc.vector.tensor_add(ko[0:16, 0:512], ko[0:16, 0:512],
                                              k2[0:16, 0:512]))
            pr.v(lambda: nc.vector.tensor_scalar_mul(ko[0:16, 0:512], ko[0:16, 0:512],
                                                     -1.0))
            pr.d(lambda t0=t0: nc.sync.dma_start(
                out=la_o[0:16, t0:t0 + CH], in_=ko[0:16, 0:512]))
        emit_mm(pr, nc, env, Wa, DM, 16, CH, rhs, a_out)
    finish(nc, pr)
    es.close()
    return nc


def build_attn():
    nc = bass.Bass("TRN2", target_bir_lowering=False)
    T = 8192
    NU = 4
    q_in = hin(nc, "q", [NU, 64, T])
    k_in = hin(nc, "k", [NU, 64, T])
    v_in = hin(nc, "v", [NU, 64, T])
    la_in = hin(nc, "la", [NU, T])
    maskT = hin(nc, "maskT", [128, 128])
    eye = hin(nc, "eye", [128, 128])
    eyeb = hin(nc, "eyeb", [128, 128], BF16)
    o_out = hout(nc, "o", [NU, 64, T])
    st_out = hout(nc, "st", [NU, 2])
    es = ExitStack()
    pr = P(nc)
    qf = es.enter_context(nc.sbuf_tensor("qf", [64, 128], F32))
    kf = es.enter_context(nc.sbuf_tensor("kf", [64, 128], F32))
    vf = es.enter_context(nc.sbuf_tensor("vf", [64, 128], F32))
    la = es.enter_context(nc.sbuf_tensor("la_s", [1, T], F32))
    cum = es.enter_context(nc.sbuf_tensor("cum", [1, T], F32))
    onesr = es.enter_context(nc.sbuf_tensor("onesr", [1, T], F32))
    ones1 = es.enter_context(nc.sbuf_tensor("ones1a", [1, 128], F32))
    ones128 = es.enter_context(nc.sbuf_tensor("o128a", [128, 1], F32))
    eye_s = es.enter_context(nc.sbuf_tensor("eye_s", [128, 128], F32))
    eyeb_s = es.enter_context(nc.sbuf_tensor("eyeb_s", [128, 128], BF16))
    mask_s = es.enter_context(nc.sbuf_tensor("mask_s", [128, 128], F32))
    qb = es.enter_context(nc.sbuf_tensor("qb", [64, 128], BF16))
    kb = es.enter_context(nc.sbuf_tensor("kb", [64, 128], BF16))
    vb = es.enter_context(nc.sbuf_tensor("vb", [64, 128], BF16))
    v_tm = es.enter_context(nc.sbuf_tensor("v_tm", [128, 64], BF16))
    ksc_tm = es.enter_context(nc.sbuf_tensor("ksc_tm", [128, 64], BF16))
    wt_f = es.enter_context(nc.sbuf_tensor("wt_f", [128, 128], F32))
    wt_b = es.enter_context(nc.sbuf_tensor("wt_b", [128, 128], BF16))
    t1t = es.enter_context(nc.sbuf_tensor("t1t", [128, 128], F32))
    dexpT = es.enter_context(nc.sbuf_tensor("dexpT", [128, 128], F32))
    qlT = es.enter_context(nc.sbuf_tensor("qlT", [64, 128], F32))
    qlb = es.enter_context(nc.sbuf_tensor("qlb", [64, 128], BF16))
    kscT = es.enter_context(nc.sbuf_tensor("kscT", [64, 128], F32))
    kscb = es.enter_context(nc.sbuf_tensor("kscb", [64, 128], BF16))
    S_sb = es.enter_context(nc.sbuf_tensor("S_sb", [64, 64], F32))
    S_bf = es.enter_context(nc.sbuf_tensor("S_bf", [64, 64], BF16))
    gcol = es.enter_context(nc.sbuf_tensor("gcol", [64, 1], F32))
    o_tm = es.enter_context(nc.sbuf_tensor("o_tm", [128, 64], F32))
    osq = es.enter_context(nc.sbuf_tensor("osq", [128, 64], F32))
    oT_sb = es.enter_context(nc.sbuf_tensor("oT_sb", [64, 128], F32))
    cumcol = es.enter_context(nc.sbuf_tensor("cumcol", [128, 1], F32))
    lamr = es.enter_context(nc.sbuf_tensor("lamr", [1, 128], F32))
    ksrow = es.enter_context(nc.sbuf_tensor("ksrow", [1, 128], F32))
    srow = es.enter_context(nc.sbuf_tensor("srow", [1, 128], F32))
    gr = es.enter_context(nc.sbuf_tensor("gr", [1, 1], F32))
    sum_acc = es.enter_context(nc.sbuf_tensor("sum_acc", [1, 64], F32))
    sq_acc = es.enter_context(nc.sbuf_tensor("sq_acc", [1, 64], F32))
    str_sb = es.enter_context(nc.sbuf_tensor("str_sb", [1, 2], F32))
    PA = es.enter_context(nc.psum_tensor("PA", [128, 128], F32))
    PB = es.enter_context(nc.psum_tensor("PB", [128, 128], F32))
    PO = es.enter_context(nc.psum_tensor("PO", [128, 64], F32))
    PTb = es.enter_context(nc.psum_tensor("PTb", [128, 128], BF16))
    PTf = es.enter_context(nc.psum_tensor("PTf", [128, 128], F32))
    PL = es.enter_context(nc.psum_tensor("PL", [64, 128], F32))
    PS = es.enter_context(nc.psum_tensor("PS", [64, 64], F32))
    PST = es.enter_context(nc.psum_tensor("PST", [1, 128], F32))
    pr.d(lambda: nc.sync.dma_start(out=eye_s[:, :], in_=eye[:, :]))
    pr.d(lambda: nc.sync.dma_start(out=eyeb_s[:, :], in_=eyeb[:, :]))
    pr.d(lambda: nc.sync.dma_start(out=mask_s[:, :], in_=maskT[:, :]))
    catt = es.enter_context(nc.sbuf_tensor("catt", [128, 2], F32))
    pr.v(lambda: nc.vector.memset(onesr[:, :], 1.0))
    pr.v(lambda: nc.vector.memset(ones1[:, :], 1.0))
    pr.v(lambda: nc.vector.memset(ones128[:, :], 1.0))
    pr.v(lambda: nc.vector.memset(catt[:, 0:1], -1.0))
    pr.v(lambda: nc.vector.memset(catt[:, 1:2], 1.0 / (T * 64.0)))
    for u in range(4):
        pr.d(lambda u=u: nc.sync.dma_start(out=la[0:1, :], in_=la_in[u:u + 1, :]))
        for c in range(4):
            c0 = c * 2048
            init = 0.0 if c == 0 else cum[0:1, c0 - 1:c0]
            pr.v(lambda c0=c0, init=init: nc.vector.tensor_tensor_scan(
                out=cum[0:1, c0:c0 + 2048], data0=onesr[0:1, c0:c0 + 2048],
                data1=la[0:1, c0:c0 + 2048], initial=init, op0=OP.mult, op1=OP.add))
        pr.v(lambda: nc.vector.memset(S_sb[:, :], 0.0))
        pr.v(lambda: nc.vector.memset(sum_acc[:, :], 0.0))
        pr.v(lambda: nc.vector.memset(sq_acc[:, :], 0.0))
        for i in range(T // 128):
            t0 = i * 128
            pr.d(lambda u=u, t0=t0: nc.sync.dma_start(
                out=qf[:, :], in_=q_in[u][0:64, t0:t0 + 128]))
            pr.d(lambda u=u, t0=t0: nc.sync.dma_start(
                out=kf[:, :], in_=k_in[u][0:64, t0:t0 + 128]))
            pr.d(lambda u=u, t0=t0: nc.sync.dma_start(
                out=vf[:, :], in_=v_in[u][0:64, t0:t0 + 128]))
            if i == 0:
                pr.v(lambda t0=t0: nc.vector.tensor_copy(out=srow[:, :],
                                                         in_=cum[0:1, t0:t0 + 128]))
                pr.v(lambda t0=t0: nc.vector.tensor_copy(
                    out=gr[:, :], in_=cum[0:1, t0 + 127:t0 + 128]))
            else:
                pr.v(lambda t0=t0: nc.vector.tensor_scalar(
                    out=srow[:, :], in0=cum[0:1, t0:t0 + 128],
                    scalar1=cum[0:1, t0 - 1:t0], scalar2=None, op0=OP.subtract))
                pr.v(lambda t0=t0: nc.vector.tensor_scalar(
                    out=gr[:, :], in0=cum[0:1, t0 + 127:t0 + 128],
                    scalar1=cum[0:1, t0 - 1:t0], scalar2=None, op0=OP.subtract))
            pr.s(lambda: nc.scalar.activation(out=lamr[:, :], in_=srow[:, :],
                                              func=AF.Exp))
            pr.v(lambda t0=t0: nc.vector.tensor_scalar(
                out=ksrow[:, :], in0=cum[0:1, t0:t0 + 128],
                scalar1=cum[0:1, t0 + 127:t0 + 128], scalar2=None, op0=OP.subtract))
            pr.s(lambda: nc.scalar.activation(out=ksrow[:, :], in_=ksrow[:, :],
                                              func=AF.Exp, scale=catt[0:1, 0:1]))
            pr.s(lambda: nc.scalar.activation(out=gr[:, :], in_=gr[:, :], func=AF.Exp))
            pr.t(lambda t0=t0: nc.tensor.matmul(
                out=PB[0:128, 0:128], lhsT=ones1[0:1, 0:128],
                rhs=cum[0:1, t0:t0 + 128], start=True, stop=True))
            pr.t(lambda t0=t0: nc.tensor.transpose(
                out=PTf[0:128, 0:1], in_=cum[0:1, t0:t0 + 128],
                identity=eye_s[0:1, 0:1]))
            pr.v(lambda: nc.vector.tensor_copy(out=cumcol[:, :], in_=PTf[0:128, 0:1]))
            pr.v(lambda: nc.vector.tensor_scalar(
                out=t1t[:, :], in0=PB[0:128, 0:128], scalar1=cumcol[:, 0:1],
                scalar2=None, op0=OP.subtract))
            pr.v(lambda: nc.vector.tensor_sub(t1t[:, :], t1t[:, :], mask_s[:, :]))
            pr.s(lambda: nc.scalar.activation(out=dexpT[:, :], in_=t1t[:, :],
                                              func=AF.Exp))
            pr.v(lambda: nc.vector.tensor_copy(out=qb[:, :], in_=qf[:, :]))
            pr.v(lambda: nc.vector.tensor_copy(out=kb[:, :], in_=kf[:, :]))
            pr.t(lambda: nc.tensor.matmul(out=PA[0:128, 0:128], lhsT=kb[:, :],
                                          rhs=qb[:, :], start=True, stop=True))
            pr.v(lambda: nc.vector.tensor_tensor(wt_f[:, :], PA[0:128, 0:128],
                                                 dexpT[:, :], op=OP.mult))
            pr.v(lambda: nc.vector.tensor_copy(out=wt_b[:, :], in_=wt_f[:, :]))
            pr.v(lambda: nc.vector.tensor_copy(out=vb[:, :], in_=vf[:, :]))
            pr.t(lambda: nc.tensor.transpose(out=PTb[0:128, 0:64], in_=vb[:, :],
                                             identity=eyeb_s[0:64, 0:64]))
            pr.v(lambda: nc.vector.tensor_copy(out=v_tm[:, :], in_=PTb[0:128, 0:64]))
            pr.t(lambda: nc.tensor.matmul(out=PL[0:64, 0:128], lhsT=ones1[0:1, 0:64],
                                          rhs=lamr[0:1, :], start=True, stop=True))
            pr.v(lambda: nc.vector.tensor_tensor(
                qlT[:, :], qf[:, :], PL[0:64, 0:128], op=OP.mult))
            pr.v(lambda: nc.vector.tensor_copy(out=qlb[:, :], in_=qlT[:, :]))
            pr.v(lambda: nc.vector.tensor_copy(out=S_bf[:, :], in_=S_sb[:, :]))
            pr.t(lambda: nc.tensor.matmul(out=PO[0:128, 0:64], lhsT=wt_b[:, :],
                                          rhs=v_tm[:, :], start=True, stop=False))
            pr.t(lambda: nc.tensor.matmul(out=PO[0:128, 0:64], lhsT=qlb[:, :],
                                          rhs=S_bf[:, :], start=False, stop=True))
            pr.t(lambda: nc.tensor.matmul(out=PL[0:64, 0:128], lhsT=ones1[0:1, 0:64],
                                          rhs=ksrow[0:1, :], start=True, stop=True))
            pr.v(lambda: nc.vector.tensor_tensor(
                kscT[:, :], kf[:, :], PL[0:64, 0:128], op=OP.mult))
            pr.v(lambda: nc.vector.tensor_copy(out=kscb[:, :], in_=kscT[:, :]))
            pr.t(lambda: nc.tensor.transpose(out=PTb[0:128, 0:64], in_=kscb[:, :],
                                             identity=eyeb_s[0:64, 0:64]))
            pr.v(lambda: nc.vector.tensor_copy(out=ksc_tm[:, :], in_=PTb[0:128, 0:64]))
            pr.t(lambda: nc.tensor.matmul(out=PS[0:64, 0:64], lhsT=ksc_tm[:, :],
                                          rhs=v_tm[:, :], start=True, stop=True))
            pr.t(lambda: nc.tensor.matmul(out=PL[0:64, 0:1], lhsT=ones1[0:1, 0:64],
                                          rhs=gr[0:1, 0:1], start=True, stop=True))
            pr.v(lambda: nc.vector.tensor_copy(out=gcol[:, :], in_=PL[0:64, 0:1]))
            pr.v(lambda: nc.vector.scalar_tensor_tensor(
                out=S_sb[:, :], in0=S_sb[:, :], scalar=gcol[:, 0:1],
                in1=PS[0:64, 0:64], op0=OP.mult, op1=OP.add))
            pr.v(lambda: nc.vector.tensor_copy(out=o_tm[:, :], in_=PO[0:128, 0:64]))
            pr.s(lambda: nc.scalar.activation(out=osq[:, :], in_=o_tm[:, :],
                                              func=AF.Square))
            pr.t(lambda: nc.tensor.matmul(out=PST[0:1, 0:64], lhsT=ones128[:, 0:1],
                                          rhs=o_tm[:, :], start=True, stop=True))
            pr.v(lambda: nc.vector.tensor_add(sum_acc[:, :], sum_acc[:, :],
                                              PST[0:1, 0:64]))
            pr.t(lambda: nc.tensor.matmul(out=PST[0:1, 64:128], lhsT=ones128[:, 0:1],
                                          rhs=osq[:, :], start=True, stop=True))
            pr.v(lambda: nc.vector.tensor_add(sq_acc[:, :], sq_acc[:, :],
                                              PST[0:1, 64:128]))
            pr.t(lambda: nc.tensor.transpose(out=PTf[0:64, 0:128], in_=o_tm[:, :],
                                             identity=eye_s[:, :]))
            pr.v(lambda: nc.vector.tensor_copy(out=oT_sb[:, :], in_=PTf[0:64, 0:128]))
            pr.d(lambda u=u, t0=t0: nc.sync.dma_start(
                out=o_out[u][0:64, t0:t0 + 128], in_=oT_sb[:, :]))
        pr.v(lambda: nc.vector.reduce_sum(str_sb[0:1, 0:1], sum_acc[0:1, 0:64],
                                          axis=AX.X))
        pr.v(lambda: nc.vector.reduce_sum(str_sb[0:1, 1:2], sq_acc[0:1, 0:64],
                                          axis=AX.X))
        pr.s(lambda: nc.scalar.activation(out=str_sb[0:1, 0:2], in_=str_sb[0:1, 0:2],
                                          func=AF.Copy, scale=catt[0:1, 1:2]))
        pr.v(lambda: nc.vector.tensor_mul(gr[:, :], str_sb[0:1, 0:1],
                                          str_sb[0:1, 0:1]))
        pr.v(lambda: nc.vector.tensor_sub(str_sb[0:1, 1:2], str_sb[0:1, 1:2],
                                          gr[:, :]))
        pr.d(lambda u=u: nc.sync.dma_start(out=st_out[u:u + 1, 0:2],
                                           in_=str_sb[0:1, 0:2]))
    finish(nc, pr)
    es.close()
    return nc


def build_owo():
    nc = bass.Bass("TRN2", target_bir_lowering=False)
    CH = 512
    o_in = hin(nc, "o", [DM, TC])
    g_in = hin(nc, "g", [DM, TC])
    x5_in = hin(nc, "x5", [DM, TC])
    mv = hin(nc, "mv", [128, 8])
    vv = hin(nc, "vv", [128, 8])
    gnw = hin(nc, "gnw", [128, 8])
    gnb = hin(nc, "gnb", [128, 8])
    Wo = hin(nc, "Wo", [DM, DM])
    x6_o = hout(nc, "x6", [DM, TC])
    es = ExitStack()
    pr = P(nc)
    env = common_env(nc, es, pr)
    o_sb = es.enter_context(nc.sbuf_tensor("oo", [128, 8 * CH], F32))
    g_sb = es.enter_context(nc.sbuf_tensor("gg", [128, 8 * CH], F32))
    x5_sb = es.enter_context(nc.sbuf_tensor("x5s", [128, 8 * CH], F32))
    og_bf = es.enter_context(nc.sbuf_tensor("og_bf", [128, 8 * CH], BF16))
    y_sb = es.enter_context(nc.sbuf_tensor("y_sb", [128, 512], F32))
    par = {}
    for nm, ap in [('mv', mv), ('vv', vv), ('gnw', gnw), ('gnb', gnb)]:
        par[nm] = es.enter_context(nc.sbuf_tensor(nm + "_s", [128, 8], F32))
        pr.d(lambda nm=nm, ap=ap: nc.sync.dma_start(out=par[nm][:, :], in_=ap[:, :]))
    pr.s(lambda: nc.scalar.activation(out=par['vv'][:, :], in_=par['vv'][:, :],
                                      func=AF.Ln, bias=env['cst'][:, 1:2]))
    pr.s(lambda: nc.scalar.activation(out=par['vv'][:, :], in_=par['vv'][:, :],
                                      func=AF.Exp, scale=env['cst'][:, 2:3]))
    for ci in range(TC // CH):
        t0 = ci * CH
        for ft in range(8):
            pr.d(lambda ft=ft, t0=t0: nc.sync.dma_start(
                out=o_sb[:, ft * CH:(ft + 1) * CH],
                in_=o_in[ft * 128:(ft + 1) * 128, t0:t0 + CH]))
            pr.d(lambda ft=ft, t0=t0: nc.sync.dma_start(
                out=g_sb[:, ft * CH:(ft + 1) * CH],
                in_=g_in[ft * 128:(ft + 1) * 128, t0:t0 + CH]))
            pr.d(lambda ft=ft, t0=t0: nc.sync.dma_start(
                out=x5_sb[:, ft * CH:(ft + 1) * CH],
                in_=x5_in[ft * 128:(ft + 1) * 128, t0:t0 + CH]))
        for ft in range(8):
            c0 = ft * CH
            pr.v(lambda ft=ft, c0=c0: nc.vector.tensor_scalar(
                out=env['scr'][:, 0:CH], in0=o_sb[:, c0:c0 + CH],
                scalar1=par['mv'][:, ft:ft + 1], scalar2=par['vv'][:, ft:ft + 1],
                op0=OP.subtract, op1=OP.mult))
            pr.v(lambda ft=ft: nc.vector.tensor_scalar(
                out=env['scr'][:, 0:CH], in0=env['scr'][:, 0:CH],
                scalar1=par['gnw'][:, ft:ft + 1], scalar2=par['gnb'][:, ft:ft + 1],
                op0=OP.mult, op1=OP.add))
            pr.v(lambda c0=c0: nc.vector.tensor_mul(
                env['scr'][:, 0:CH], env['scr'][:, 0:CH], g_sb[:, c0:c0 + CH]))
            pr.v(lambda c0=c0: nc.vector.tensor_copy(
                out=og_bf[:, c0:c0 + CH], in_=env['scr'][:, 0:CH]))

        def y_out(mt, sub, ps):
            pr.v(lambda mt=mt, ps=ps: nc.vector.tensor_tensor(
                out=y_sb[:, 0:512], in0=ps, in1=x5_sb[:, mt * CH:mt * CH + 512],
                op=OP.add))
            pr.d(lambda mt=mt, t0=t0: nc.sync.dma_start(
                out=x6_o[mt * 128:(mt + 1) * 128, t0:t0 + CH], in_=y_sb[:, 0:512]))
        emit_mm(pr, nc, env, Wo, DM, DM, CH,
                lambda kb, sub: og_bf[:, kb * CH:kb * CH + 512], y_out)
    finish(nc, pr)
    es.close()
    return nc


_cache = {}


def _get(name, builder):
    if name not in _cache:
        _cache[name] = builder()
    return _cache[name]


def _run(nc, maps):
    res = run_bass_kernel_spmd(nc, maps, core_ids=list(range(NCORE)))
    return res.results


def _col8(vec):
    return np.ascontiguousarray(np.asarray(vec, np.float32).reshape(8, 128).T)


def _cat(results, key):
    return np.concatenate([r[key] for r in results], axis=1)


def kernel(**inp):
    f32 = np.float32
    TP = lambda a: np.ascontiguousarray(np.asarray(a, f32).T)
    C = lambda a: np.ascontiguousarray(np.asarray(a, f32))
    xT = np.ascontiguousarray(np.asarray(inp['L0_attn_x'], f32).reshape(TT, DM).T)
    eye = np.eye(128, dtype=f32)
    eyeb = np.eye(128, dtype=np.float32).astype(BF)
    ii = np.arange(128)
    maskT = np.where(ii[:, None] <= ii[None, :], 0.0, 1e9).astype(f32)

    def sel_const(E):
        s = np.zeros((E, E, 128), np.float32)
        for e in range(E):
            s[e, e, :] = 1.0
        return s.astype(BF)

    def core_slices(arr):
        return [np.ascontiguousarray(arr[:, c * TC:(c + 1) * TC]) for c in range(NCORE)]

    w_ffn0 = {
        'sgT': TP(inp['L0_ffn_shared_gate']), 'suT': TP(inp['L0_ffn_shared_up']),
        'sdT': TP(inp['L0_ffn_shared_down']),
        'rgT': np.ascontiguousarray(np.transpose(np.asarray(inp['L0_ffn_routed_gate'], f32), (0, 2, 1))),
        'ruT': np.ascontiguousarray(np.transpose(np.asarray(inp['L0_ffn_routed_up'], f32), (0, 2, 1))),
        'rdT': np.ascontiguousarray(np.transpose(np.asarray(inp['L0_ffn_routed_down'], f32), (0, 2, 1))),
        'rwT': TP(inp['L0_ffn_router_w']), 'selE': sel_const(7), 'eye': eye,
    }
    w1 = np.asarray(inp['L1_ffn_w1'], f32)
    w_ffn1 = {
        'rgT': np.ascontiguousarray(np.transpose(w1[:, :2048, :], (0, 2, 1))),
        'ruT': np.ascontiguousarray(np.transpose(w1[:, 2048:, :], (0, 2, 1))),
        'rdT': np.ascontiguousarray(np.transpose(np.asarray(inp['L1_ffn_w2'], f32), (0, 2, 1))),
        'rwT': TP(inp['L1_ffn_router_w']), 'selE': sel_const(8), 'eye': eye,
    }

    nc_ffn0 = _get('ffn0', lambda: build_ffn('ffn0'))
    nc_ffn1 = _get('ffn1', lambda: build_ffn('ffn1'))

    def run_ffn(nc, w, xTfull):
        xs = core_slices(xTfull)
        maps = [dict(w, x=xs[c]) for c in range(NCORE)]
        return _cat(_run(nc, maps), 'y')

    x1 = run_ffn(nc_ffn0, w_ffn0, xT)

    nc_A = _get('rwkvA', build_rwkvA)
    zcol = np.zeros((128, 8), f32)
    mapsA = []
    for c in range(NCORE):
        first = (c % 4 == 0)
        mapsA.append({
            'x': np.ascontiguousarray(x1[:, c * TC:(c + 1) * TC]),
            'xprev': zcol if first else _col8(x1[:, c * TC - 1]),
            'fmask': np.full((128, 1), 0.0 if first else 1.0, f32),
            'ln0g': _col8(inp['L0_attn_ln0_g']), 'ln0b': _col8(inp['L0_attn_ln0_b']),
            'preg': _col8(inp['L0_attn_pre_g']), 'preb': _col8(inp['L0_attn_pre_b']),
            'mixk': _col8(inp['L0_attn_tm_mix_k']), 'mixv': _col8(inp['L0_attn_tm_mix_v']),
            'mixr': _col8(inp['L0_attn_tm_mix_r']),
            'Wk': C(inp['L0_attn_tm_Wk']), 'Wv': C(inp['L0_attn_tm_Wv']),
            'Wr': C(inp['L0_attn_tm_Wr']),
        })
    resA = _run(nc_A, mapsA)
    kT = _cat(resA, 'kT')
    vT = _cat(resA, 'vT')
    srT = _cat(resA, 'srT')
    t0T = _cat(resA, 't0')

    nc_S = _get('scan', build_scan)
    dec = np.asarray(inp['L0_attn_tm_decay'], f32)
    fir = np.asarray(inp['L0_attn_tm_first'], f32)
    mapsS = []
    for c in range(NCORE):
        r0 = c * 512
        mapsS.append({
            'k': np.ascontiguousarray(kT[r0:r0 + 512, :]),
            'v': np.ascontiguousarray(vT[r0:r0 + 512, :]),
            'dec': np.ascontiguousarray(dec[r0:r0 + 512].reshape(4, 128).T),
            'fir': np.ascontiguousarray(fir[r0:r0 + 512].reshape(4, 128).T),
        })
    resS = _run(nc_S, mapsS)
    ratT = np.concatenate([r['rat'] for r in resS], axis=0)

    nc_B = _get('rwkvB', build_rwkvB)
    mapsB = [{'rat': np.ascontiguousarray(ratT[:, c * TC:(c + 1) * TC]),
              'sr': np.ascontiguousarray(srT[:, c * TC:(c + 1) * TC]),
              't0': np.ascontiguousarray(t0T[:, c * TC:(c + 1) * TC]),
              'Wo': C(inp['L0_attn_tm_Wo'])} for c in range(NCORE)]
    t1T = _cat(_run(nc_B, mapsB), 't1')

    nc_C = _get('chanmix', build_chanmix)
    mapsC = []
    for c in range(NCORE):
        first = (c % 4 == 0)
        mapsC.append({
            't1': np.ascontiguousarray(t1T[:, c * TC:(c + 1) * TC]),
            'x1': np.ascontiguousarray(x1[:, c * TC:(c + 1) * TC]),
            'tprev': zcol if first else _col8(t1T[:, c * TC - 1]),
            'fmask': np.full((128, 1), 0.0 if first else 1.0, f32),
            'postg': _col8(inp['L0_attn_post_g']), 'postb': _col8(inp['L0_attn_post_b']),
            'cmixk': _col8(inp['L0_attn_cm_mix_k']), 'cmixr': _col8(inp['L0_attn_cm_mix_r']),
            'cWk': C(inp['L0_attn_cm_Wk']), 'cWv': C(inp['L0_attn_cm_Wv']),
            'cWr': C(inp['L0_attn_cm_Wr']),
        })
    x3 = _cat(_run(nc_C, mapsC), 'x3')

    x4 = run_ffn(nc_ffn0, w_ffn0, x3)
    x5 = run_ffn(nc_ffn1, w_ffn1, x4)

    nc_Q = _get('qkvg', build_qkvg)
    wq = {'Wq': C(inp['L1_attn_Wq']), 'Wk': C(inp['L1_attn_Wk']),
          'Wv': C(inp['L1_attn_Wv']), 'Wg': C(inp['L1_attn_Wg']),
          'Wa': C(inp['L1_attn_Wa']),
          'ba': np.ascontiguousarray(np.asarray(inp['L1_attn_ba'], f32).reshape(16, 1))}
    mapsQ = [dict(wq, x=np.ascontiguousarray(x5[:, c * TC:(c + 1) * TC]))
             for c in range(NCORE)]
    resQ = _run(nc_Q, mapsQ)
    QT = _cat(resQ, 'q')
    KT = _cat(resQ, 'k')
    VT = _cat(resQ, 'v')
    GT = _cat(resQ, 'g')
    laT = _cat(resQ, 'la')

    nc_T = _get('attn', build_attn)
    mapsT = []
    for c in range(NCORE):
        qs, ks, vs, ls = [], [], [], []
        for j in range(4):
            u = c * 4 + j
            b, h = u // 16, u % 16
            sl = np.s_[h * 64:(h + 1) * 64, b * 8192:(b + 1) * 8192]
            qs.append(QT[sl]); ks.append(KT[sl]); vs.append(VT[sl])
            ls.append(laT[h, b * 8192:(b + 1) * 8192])
        mapsT.append({'q': np.ascontiguousarray(np.stack(qs)),
                      'k': np.ascontiguousarray(np.stack(ks)),
                      'v': np.ascontiguousarray(np.stack(vs)),
                      'la': np.ascontiguousarray(np.stack(ls)),
                      'maskT': maskT, 'eye': eye, 'eyeb': eyeb})
    resT = _run(nc_T, mapsT)
    oT = np.zeros((DM, TT), f32)
    stats = np.zeros((2, 16, 2), f32)
    for c in range(NCORE):
        for j in range(4):
            u = c * 4 + j
            b, h = u // 16, u % 16
            oT[h * 64:(h + 1) * 64, b * 8192:(b + 1) * 8192] = resT[c]['o'][j]
            stats[b, h] = resT[c]['st'][j]

    nc_O = _get('owo', build_owo)
    mapsO = []
    for c in range(NCORE):
        b = c // 4
        mv = np.repeat(stats[b, :, 0], 64)
        vvec = np.repeat(stats[b, :, 1], 64)
        mapsO.append({
            'o': np.ascontiguousarray(oT[:, c * TC:(c + 1) * TC]),
            'g': np.ascontiguousarray(GT[:, c * TC:(c + 1) * TC]),
            'x5': np.ascontiguousarray(x5[:, c * TC:(c + 1) * TC]),
            'mv': _col8(mv), 'vv': _col8(vvec),
            'gnw': _col8(inp['L1_attn_gn_w']), 'gnb': _col8(inp['L1_attn_gn_b']),
            'Wo': C(inp['L1_attn_Wo']),
        })
    x6 = _cat(_run(nc_O, mapsO), 'x6')

    x7 = run_ffn(nc_ffn1, w_ffn1, x6)
    return np.ascontiguousarray(x7.T).reshape(2, 8192, DM).astype(np.float32)
